# version 02
# baseline (speedup 1.0000x reference)
import contextlib

import numpy as np
import ml_dtypes

import concourse.bass as bass
import concourse.mybir as mybir
from concourse.bass_utils import run_bass_kernel_spmd

F32 = mybir.dt.float32
BF16 = mybir.dt.bfloat16
AF = mybir.ActivationFunctionType
ALU = mybir.AluOpType
AX = mybir.AxisListType

C = 4096
CCN = 32
T = 2048
MY = 1024
HL = 16
HD = 128

DEBUG = False
ATT_PIPE = True

CONSTS = [-20.0, 0.5, 0.08838834764831845, 1.0, 0.0, -1e30, 1e-6, 1e-9, 2.0,
          1.0 / 4096.0, 1e-5, 1.5, -2.0]


class Sem:
    def __init__(self, h):
        self.h = h
        self.v = 0

    def inc(self, n=1):
        self.v += n
        return self.v

    def now(self):
        return (self.h, self.v)

    def at(self, v):
        return (self.h, v)


def build_nc(debug=False):
    nc = bass.Bass("TRN2", target_bir_lowering=False)

    def din(name, shape, dt=F32):
        return nc.dram_tensor(name, shape, dt, kind="ExternalInput").ap()

    x_fmT = din("x_fmT", [C, T])
    x_myT = din("x_myT", [C, MY])
    wvT = din("wvT", [C, HL * HD])
    wkTp = din("wkTp", [HL * 4, 128, 1024])
    wqTp = din("wqTp", [HL * 4, 128, 1024])
    woTp = din("woTp", [CCN * 2, 128, 1024])
    WTp = din("WTp", [CCN * 4, 128, 1024])
    b_re = din("b_re", [128, CCN])
    A_in = din("A_in", [C, 32])
    Bm_in = din("Bm_in", [32, C])
    Lm_in = din("Lm_in", [8, 32])
    LmT_in = din("LmT_in", [32, 8])
    cos_in = din("cos_in", [128, T], BF16)
    sin_in = din("sin_in", [128, T], BF16)
    mask_in = din("mask_in", [128, 128], BF16)
    ident_in = din("ident_in", [128, 128])

    out_fm = nc.dram_tensor("out_fm", [C, MY], F32, kind="ExternalOutput").ap()
    dbg = {}
    if debug:
        def dout(name, shape, dt=F32):
            dbg[name] = nc.dram_tensor(name, shape, dt, kind="ExternalOutput").ap()
        dout("d_xn0", [128, MY], BF16)
        dout("d_xnf0", [128, T], BF16)
        dout("d_H0", [128, MY])
        dout("d_z", [32, MY])
        dout("d_sc", [1, 2])
        dout("d_k0", [128, T], BF16)
        dout("d_q0", [128, T], BF16)
        dout("d_v0", [T, 128], BF16)
        dout("d_o0", [128, T], BF16)
        dout("d_sp", [128, MY])
        dout("d_zpb", [32, MY], BF16)
        dout("d_wn", [8, 512])
        dout("d_kraw", [128, 2048], BF16)
        dout("d_ksw", [128, 2048], BF16)
        dout("d_rr", [8, 2])
        dout("d_ag", [1, 64])
        dout("d_hs", [8, 512])
        dout("d_pt0", [128, 1024], BF16)
        dout("d_sum", [1, 1024])
        dout("d_g0", [128, MY])
        dout("d_g1", [128, MY])

    q_dram = nc.dram_tensor("q_dram", [HL, HD, T], BF16).ap()
    k_dram = nc.dram_tensor("k_dram", [HL, HD, T], BF16).ap()
    v_dram = nc.dram_tensor("v_dram", [HL, T, HD], BF16).ap()
    h_dram = nc.dram_tensor("h_dram", [C, MY], F32).ap()
    sp_in_t = nc.dram_tensor("sp_in", [2 * C, MY], F32)
    sp_out_t = nc.dram_tensor("sp_out", [C, MY], F32)
    g_dram = nc.dram_tensor("g_dram", [C, MY], F32).ap()
    ag_in_t = nc.dram_tensor("ag_in", [1, 8], F32)
    ag_out_t = nc.dram_tensor("ag_out", [8, 8], F32)
    red_d = nc.dram_tensor("red_d", [16, 1], F32).ap()
    sp_in = sp_in_t.ap()
    sp_out = sp_out_t.ap()
    ag_in = ag_in_t.ap()
    ag_out = ag_out_t.ap()

    es = contextlib.ExitStack()
    with es:
        XN = es.enter_context(nc.sbuf_tensor("XN", [128, 65536], BF16))
        ABF = es.enter_context(nc.sbuf_tensor("ABF", [128, 18432], BF16))
        AFR = es.enter_context(nc.sbuf_tensor("AFR", [128, 5120], F32))
        consts = es.enter_context(nc.sbuf_tensor("consts", [128, 16], F32))
        onesc = es.enter_context(nc.sbuf_tensor("onesc", [128, 1], BF16))
        ones_rowf = es.enter_context(nc.sbuf_tensor("ones_rowf", [1, 128], F32))
        ones8 = es.enter_context(nc.sbuf_tensor("ones8", [1, 8], F32))
        b_sb = es.enter_context(nc.sbuf_tensor("b_sb", [128, CCN], F32))
        zpb = es.enter_context(nc.sbuf_tensor("zpb", [32, MY], BF16))
        hs_sb = es.enter_context(nc.sbuf_tensor("hs_sb", [8, 512], F32))
        ds_sb = es.enter_context(nc.sbuf_tensor("ds_sb", [8, 512], F32))
        mask_sb = es.enter_context(nc.sbuf_tensor("mask_sb", [128, 128], BF16))
        Lmb = es.enter_context(nc.sbuf_tensor("Lmb", [8, 32], BF16))
        LmTb = es.enter_context(nc.sbuf_tensor("LmTb", [32, 8], BF16))
        agsb = es.enter_context(nc.sbuf_tensor("agsb", [1, 64], F32))
        mx2 = es.enter_context(nc.sbuf_tensor("mx2", [1, 2], F32))
        sc2 = es.enter_context(nc.sbuf_tensor("sc2", [8, 2], F32))
        rr2 = es.enter_context(nc.sbuf_tensor("rr2", [8, 2], F32))
        PSB = [es.enter_context(nc.psum_tensor(f"PSB{i}", [128, 512], F32))
               for i in range(8)]
        s_dW = Sem(es.enter_context(nc.semaphore("dW")))
        s_dS = Sem(es.enter_context(nc.semaphore("dS")))
        s_dV = Sem(es.enter_context(nc.semaphore("dV")))
        s_pe = Sem(es.enter_context(nc.semaphore("pe")))
        s_ve = Sem(es.enter_context(nc.semaphore("ve")))
        s_sc = Sem(es.enter_context(nc.semaphore("sc")))
        s_po = Sem(es.enter_context(nc.semaphore("po")))
        s_cg = Sem(es.enter_context(nc.semaphore("cg")))
        block = es.enter_context(nc.Block())

        for i, cv in enumerate(CONSTS):
            nc.const_aps.register(F32, cv, consts[:, i:i + 1])
        KN = {cv: i for i, cv in enumerate(CONSTS)}

        plan = {"sync": [], "vector": [], "scalar": [], "tensor": [], "gpsimd": []}

        def op(q, fn, waits=(), inc=None):
            waits = tuple(waits)

            def run(e, fn=fn, waits=waits, inc=inc):
                for (sh, v) in waits:
                    e.wait_ge(sh, v)
                r = fn(e)
                if inc is not None:
                    r.then_inc(*inc)
            plan[q].append(run)

        def dbg_dump(name, src_ap):
            if not debug:
                return
            op("gpsimd", lambda e, d=dbg[name], s=src_ap: e.dma_start(
                out=d, in_=s), inc=(s_dS.h, 16))
            s_dS.inc(16)

        def barrier():
            vals = [(s.h, s.v) for s in (s_dW, s_dS, s_dV, s_pe, s_ve, s_sc, s_po)]
            for q in plan:
                op(q, lambda e: None, waits=vals)

        S = [AFR[:, i * 1024:(i + 1) * 1024] for i in range(5)]

        def afr(i, lo, hi):
            return AFR[:, i * 1024 + lo:i * 1024 + hi]

        for i, cv in enumerate(CONSTS):
            op("gpsimd", lambda e, i=i, cv=cv: e.memset(consts[:, i:i + 1], cv),
               inc=(s_po.h, 1)); s_po.inc()
        op("gpsimd", lambda e: e.memset(onesc[:], 1.0), inc=(s_po.h, 1)); s_po.inc()
        op("gpsimd", lambda e: e.memset(ones_rowf[:], 1.0), inc=(s_po.h, 1)); s_po.inc()
        op("gpsimd", lambda e: e.memset(ones8[:], 1.0), inc=(s_po.h, 1)); s_po.inc()
        po_init = s_po.v
        for dst, src in [(b_sb[:], b_re), (mask_sb[:], mask_in)]:
            op("sync", lambda e, d=dst, s=src: e.dma_start(out=d, in_=s),
               inc=(s_dW.h, 16)); s_dW.inc(16)
        op("sync", lambda e: e.dma_start(out=afr(4, 0, 32)[0:8, :], in_=Lm_in),
           inc=(s_dW.h, 16)); s_dW.inc(16)
        op("sync", lambda e: e.dma_start(out=afr(4, 32, 40)[0:32, :], in_=LmT_in),
           inc=(s_dW.h, 16)); s_dW.inc(16)
        dW_setup = s_dW.v
        op("vector", lambda e: e.tensor_copy(Lmb[:], afr(4, 0, 32)[0:8, :]),
           waits=[s_dW.at(dW_setup)], inc=(s_ve.h, 1)); s_ve.inc()
        op("vector", lambda e: e.tensor_copy(LmTb[:], afr(4, 32, 40)[0:32, :]),
           inc=(s_ve.h, 1)); s_ve.inc()
        barrier()

        def emit_ln(x_src, xn_dst, ncols):
            ng = (ncols + 511) // 512
            assert ncols == 1024
            xst = [S[0], S[1]]
            xbq = [ABF[:, 0:1024], ABF[:, 1024:2048]]
            xsq = [ABF[:, 2048:3072], ABF[:, 3072:4096]]
            mb, rb = S[2], S[3]
            last_cast = [s_ve.v, s_ve.v]
            last_sq = [s_sc.v, s_sc.v]
            last_pe_read = [s_pe.v, s_pe.v]
            for cc in range(CCN):
                bi = cc % 2
                wl = [s_ve.at(last_cast[bi]), s_sc.at(last_sq[bi])]
                op("sync", lambda e, cc=cc, bi=bi: e.dma_start(
                    out=xst[bi][:], in_=x_src(cc)), waits=wl, inc=(s_dW.h, 16))
                dv = s_dW.inc(16)
                op("vector", lambda e, bi=bi: e.tensor_copy(xbq[bi][:], xst[bi][:]),
                   waits=[s_dW.at(dv), s_pe.at(last_pe_read[bi])], inc=(s_ve.h, 1))
                last_cast[bi] = s_ve.inc()
                op("scalar", lambda e, bi=bi: e.activation(
                    xsq[bi][:], xst[bi][:], AF.Square),
                   waits=[s_dW.at(dv), s_pe.at(last_pe_read[bi])], inc=(s_sc.h, 1))
                last_sq[bi] = s_sc.inc()
                def mms(e, cc=cc, bi=bi):
                    e.matmul(PSB[0][0:1, :], onesc[:], xbq[bi][:, 0:512],
                             start=(cc == 0), stop=(cc == CCN - 1))
                    e.matmul(PSB[1][0:1, :], onesc[:], xbq[bi][:, 512:1024],
                             start=(cc == 0), stop=(cc == CCN - 1))
                    e.matmul(PSB[2][0:1, :], onesc[:], xsq[bi][:, 0:512],
                             start=(cc == 0), stop=(cc == CCN - 1))
                    return e.matmul(PSB[3][0:1, :], onesc[:], xsq[bi][:, 512:1024],
                                    start=(cc == 0), stop=(cc == CCN - 1))
                op("tensor", mms,
                   waits=[s_ve.at(last_cast[bi]), s_sc.at(last_sq[bi]),
                          s_po.at(po_init)], inc=(s_pe.h, 1))
                last_pe_read[bi] = s_pe.inc()
            pe_stats = s_pe.v
            s0r = S[0][0:1, :]
            s1r = S[1][0:1, :]
            s2r = S[2][0:1, :]

            def rows(e):
                e.tensor_copy(s0r[:, 0:512], PSB[0][0:1, :])
                e.tensor_copy(s0r[:, 512:1024], PSB[1][0:1, :])
                e.tensor_copy(s1r[:, 0:512], PSB[2][0:1, :])
                e.tensor_copy(s1r[:, 512:1024], PSB[3][0:1, :])
                e.tensor_scalar(s0r, s0r, 1.0 / 4096.0, None, op0=ALU.mult)
                e.tensor_scalar(s1r, s1r, 1.0 / 4096.0, None, op0=ALU.mult)
                e.tensor_tensor(s2r, s0r, s0r, op=ALU.mult)
                e.tensor_tensor(s1r, s1r, s2r, op=ALU.subtract)
                return e.tensor_scalar(s1r, s1r, 1e-5, None, op0=ALU.add)
            op("vector", rows, waits=[s_pe.at(pe_stats)], inc=(s_ve.h, 1))
            ve_rows = s_ve.inc()
            op("scalar", lambda e: e.activation(s2r, s1r, AF.Sqrt),
               waits=[s_ve.at(ve_rows)], inc=(s_sc.h, 1))
            sc_sqrt = s_sc.inc()
            op("vector", lambda e: e.reciprocal(s1r, s2r),
               waits=[s_sc.at(sc_sqrt)], inc=(s_ve.h, 1))
            ve_r = s_ve.inc()

            def bcast(e):
                e.matmul(PSB[4][:, :], ones_rowf[:], s0r[:, 0:512], start=True, stop=True)
                e.matmul(PSB[5][:, :], ones_rowf[:], s0r[:, 512:1024], start=True, stop=True)
                e.matmul(PSB[6][:, :], ones_rowf[:], s1r[:, 0:512], start=True, stop=True)
                return e.matmul(PSB[7][:, :], ones_rowf[:], s1r[:, 512:1024],
                                start=True, stop=True)
            op("tensor", bcast, waits=[s_ve.at(ve_r)], inc=(s_pe.h, 1))
            pe_bc = s_pe.inc()

            def mbrb(e):
                e.copy(mb[:, 0:512], PSB[4][:, :])
                e.copy(mb[:, 512:1024], PSB[5][:, :])
                e.copy(rb[:, 0:512], PSB[6][:, :])
                return e.copy(rb[:, 512:1024], PSB[7][:, :])
            op("scalar", mbrb, waits=[s_pe.at(pe_bc)], inc=(s_sc.h, 1))
            sc_mbrb = s_sc.inc()
            last_sub = [max(ve_r, ve_rows), max(ve_r, ve_rows)]
            for cc in range(CCN):
                bi = cc % 2
                op("sync", lambda e, cc=cc, bi=bi: e.dma_start(
                    out=xst[bi][:], in_=x_src(cc)),
                   waits=[s_ve.at(last_sub[bi]), s_pe.at(pe_bc)], inc=(s_dW.h, 16))
                dv = s_dW.inc(16)

                def xn2(e, cc=cc, bi=bi):
                    e.tensor_tensor(xst[bi][:], xst[bi][:], mb[:], op=ALU.subtract)
                    return e.tensor_tensor(xn_dst(cc), xst[bi][:], rb[:], op=ALU.mult)
                op("vector", xn2, waits=[s_dW.at(dv), s_sc.at(sc_mbrb)],
                   inc=(s_ve.h, 1))
                last_sub[bi] = s_ve.inc()

        xn_my = lambda cc: XN[:, cc * MY:(cc + 1) * MY]
        emit_ln(lambda cc: x_myT[cc * 128:(cc + 1) * 128, :], xn_my, MY)
        barrier()
        dbg_dump("d_xn0", XN[:, 0:MY])

        z_sb = S[4][0:32, :]
        ast = S[0]
        Abf = ABF[0:128, 0:1024]
        zbf = ABF[0:32, 1024:2048]
        zanb = ABF[0:32, 2048:2560]
        op("sync", lambda e: e.dma_start(
            out=ast.rearrange("p (cc r) -> p cc r", cc=CCN),
            in_=A_in.rearrange("(cc p) r -> p cc r", p=128)), inc=(s_dW.h, 16))
        dv = s_dW.inc(16)
        op("vector", lambda e: e.tensor_copy(Abf[:], ast[:]),
           waits=[s_dW.at(dv)], inc=(s_ve.h, 1))
        ve_abf = s_ve.inc()

        def zmms(e):
            for cc in range(CCN):
                e.ldweights(Abf[:, cc * 32:(cc + 1) * 32])
                e.matmul(PSB[4][0:32, :], Abf[:, cc * 32:(cc + 1) * 32],
                         xn_my(cc)[:, 0:512], start=(cc == 0), stop=(cc == CCN - 1))
                r = e.matmul(PSB[5][0:32, :], Abf[:, cc * 32:(cc + 1) * 32],
                             xn_my(cc)[:, 512:1024], start=(cc == 0), stop=(cc == CCN - 1))
            return r
        op("tensor", zmms, waits=[s_ve.at(ve_abf)], inc=(s_pe.h, 1))
        pe_z = s_pe.inc()

        def zev(e):
            e.tensor_copy(z_sb[:, 0:512], PSB[4][0:32, :])
            e.tensor_copy(z_sb[:, 512:1024], PSB[5][0:32, :])
            e.tensor_copy(zbf[:], z_sb[:])
            return e.tensor_copy(zanb[:], zbf[:, 1:1024:2])
        op("vector", zev, waits=[s_pe.at(pe_z)], inc=(s_ve.h, 1))
        ve_z = s_ve.inc()

        def dsmm(e):
            e.ldweights(LmTb[:])
            return e.matmul(PSB[6][0:8, :], LmTb[:], zanb[:], start=True, stop=True)
        op("tensor", dsmm, waits=[s_ve.at(ve_z)], inc=(s_pe.h, 1))
        pe_ds = s_pe.inc()
        op("vector", lambda e: e.tensor_copy(ds_sb[:], PSB[6][0:8, :]),
           waits=[s_pe.at(pe_ds)], inc=(s_ve.h, 1))
        s_ve.inc()
        barrier()
        dbg_dump("d_z", z_sb[:])

        wst = [S[0], S[1]]
        hsb = [S[2], S[3]]
        wbfH = [ABF[:, 0:4096], ABF[:, 4096:8192]]
        last_wcast = {}
        last_hsb_dma = [s_dS.v, s_dS.v]
        last_evac = [s_sc.v, s_sc.v]
        last_pe_wbf = [s_pe.v, s_pe.v]
        qcast = {}
        for oc in range(CCN):
            par = oc % 2
            for q in range(4):
                bi = (oc * 4 + q) % 2
                wl = [s_ve.at(last_wcast.get(bi, 0))]
                op("sync", lambda e, oc=oc, q=q, bi=bi: e.dma_start(
                    out=wst[bi][:], in_=WTp[oc * 4 + q, :, :]),
                   waits=wl, inc=(s_dW.h, 16))
                dv = s_dW.inc(16)
                op("vector", lambda e, par=par, q=q, bi=bi: e.tensor_copy(
                    wbfH[par][:, q * 1024:(q + 1) * 1024], wst[bi][:]),
                   waits=[s_dW.at(dv), s_pe.at(last_pe_wbf[par])], inc=(s_ve.h, 1))
                last_wcast[bi] = s_ve.inc()
                qcast[(oc, q)] = last_wcast[bi]
            p0, p1 = (0, 1) if par == 0 else (2, 3)

            def hmm(e, oc=oc, par=par, p0=p0, p1=p1):
                for cc in range(CCN):
                    e.ldweights(wbfH[par][:, cc * 128:(cc + 1) * 128])
                    e.matmul(PSB[p0][:, :], wbfH[par][:, cc * 128:(cc + 1) * 128],
                             xn_my(cc)[:, 0:512], start=(cc == 0), stop=(cc == CCN - 1))
                    r = e.matmul(PSB[p1][:, :], wbfH[par][:, cc * 128:(cc + 1) * 128],
                                 xn_my(cc)[:, 512:1024], start=(cc == 0),
                                 stop=(cc == CCN - 1))
                return r
            op("tensor", hmm,
               waits=[s_ve.at(qcast[(oc, 3)]), s_sc.at(last_evac[par])],
               inc=(s_pe.h, 1))
            pe_h = s_pe.inc()
            last_pe_wbf[par] = pe_h

            def hev(e, oc=oc, par=par, p0=p0, p1=p1):
                e.copy(hsb[par][:, 0:512], PSB[p0][:, :])
                return e.copy(hsb[par][:, 512:1024], PSB[p1][:, :])
            op("scalar", hev,
               waits=[s_pe.at(pe_h), s_dS.at(last_hsb_dma[par])], inc=(s_sc.h, 1))
            last_evac[par] = s_sc.inc()
            op("gpsimd", lambda e, oc=oc, par=par: e.dma_start(
                out=h_dram[oc * 128:(oc + 1) * 128, :], in_=hsb[par][:]),
               waits=[s_sc.at(last_evac[par])], inc=(s_dS.h, 16))
            last_hsb_dma[par] = s_dS.inc(16)
            if oc == 0:
                def hsx(e):
                    e.tensor_copy(hs_sb[:], hsb[0][0:8, 1:1024:2])
                    return e.tensor_scalar(hs_sb[:], hs_sb[:], b_sb[0:8, 0:1],
                                           None, op0=ALU.add)
                op("vector", hsx,
                   waits=[s_sc.at(last_evac[0])], inc=(s_ve.h, 1))
                s_ve.inc()
        barrier()

        def rmx(e):
            e.tensor_reduce(rr2[:, 0:1], hs_sb[:], axis=AX.X, op=ALU.max,
                            apply_absolute_value=True)
            return e.tensor_reduce(rr2[:, 1:2], ds_sb[:], axis=AX.X, op=ALU.max,
                                   apply_absolute_value=True)
        op("vector", rmx, inc=(s_ve.h, 1))
        ve_r = s_ve.inc()
        op("vector", lambda e: e.tensor_copy(XN[:, 65024:65280],
                                             XN[:, 65280:65536]),
           waits=[s_ve.at(ve_r)], inc=(s_ve.h, 1))
        ve_r = s_ve.inc()
        op("gpsimd", lambda e: e.dma_start(
            out=red_d.rearrange("(a b) one -> a (b one)", b=2), in_=rr2[:]),
           waits=[s_ve.at(ve_r)], inc=(s_dS.h, 16))
        dv = s_dS.inc(16)
        op("gpsimd", lambda e: e.dma_start(
            out=agsb[0:1, 0:16], in_=red_d.rearrange("a one -> one a")),
           waits=[s_dS.at(dv)], inc=(s_dS.h, 16))
        dv = s_dS.inc(16)

        def gmax(e):
            e.tensor_reduce(mx2[0:1, 0:1], agsb[0:1, 0:16:2], axis=AX.X, op=ALU.max)
            e.tensor_reduce(mx2[0:1, 1:2], agsb[0:1, 1:16:2], axis=AX.X, op=ALU.max)
            e.tensor_copy(XN[:, 65024:65280], XN[:, 65280:65536])
            e.tensor_scalar(mx2[:], mx2[:], 1e-6, None, op0=ALU.max)
            return e.tensor_copy(XN[:, 65024:65280], XN[:, 65280:65536])
        op("vector", gmax, waits=[s_dS.at(dv)], inc=(s_ve.h, 1))
        ve_lmax = s_ve.inc()
        op("gpsimd", lambda e: e.dma_start(out=ag_in[0:1, 0:2], in_=mx2[0:1, 0:2]),
           waits=[s_ve.at(ve_lmax)], inc=(s_dS.h, 16))
        dv = s_dS.inc(16)
        op("gpsimd", lambda e: e.collective_compute(
            "AllGather", ALU.bypass,
            replica_groups=[[0, 1, 2, 3, 4, 5, 6, 7]],
            ins=[ag_in_t.ap().opt()], outs=[ag_out_t.ap().opt()]),
           waits=[s_dS.at(dv)], inc=(s_cg.h, 1))
        s_cg.inc()
        barrier()

        xn_full = lambda cc: XN[:, cc * T:(cc + 1) * T]
        for u in range(2):
            emit_ln(lambda cc, u=u: x_fmT[cc * 128:(cc + 1) * 128,
                                          u * 1024:(u + 1) * 1024],
                    lambda cc, u=u: XN[:, cc * T + u * 1024: cc * T + (u + 1) * 1024],
                    MY)
            barrier()
        dbg_dump("d_xnf0", XN[:, 0:T])

        wvb = ABF[:, 0:16384]
        vwst = [S[0][:, 0:512], S[1][:, 0:512]]
        vev = [ABF[:, 16384:16896], ABF[:, 16896:17408]]
        last_vwcast = {}
        vcast = {}
        pe_end_g = s_pe.v
        last_vev_dma = [s_dS.v, s_dS.v]
        last_vev_sc = [s_sc.v, s_sc.v]
        for g in range(4):
            for cc in range(CCN):
                bi = (g * CCN + cc) % 2
                op("sync", lambda e, g=g, cc=cc, bi=bi: e.dma_start(
                    out=vwst[bi], in_=wvT[cc * 128:(cc + 1) * 128,
                                          g * 512:(g + 1) * 512]),
                   waits=[s_ve.at(last_vwcast.get(bi, 0))], inc=(s_dW.h, 16))
                dv = s_dW.inc(16)
                op("vector", lambda e, cc=cc, bi=bi: e.tensor_copy(
                    wvb[:, cc * 512:(cc + 1) * 512], vwst[bi]),
                   waits=[s_dW.at(dv), s_pe.at(pe_end_g)], inc=(s_ve.h, 1))
                last_vwcast[bi] = s_ve.inc()
                vcast[(g, cc)] = last_vwcast[bi]
            for tt in range(16):
                pb = tt % 2

                def vmm(e, g=g, tt=tt, pb=pb):
                    for cc in range(CCN):
                        e.ldweights(XN[:, cc * T + tt * 128: cc * T + (tt + 1) * 128])
                        r = e.matmul(PSB[pb][:, :],
                                     XN[:, cc * T + tt * 128: cc * T + (tt + 1) * 128],
                                     wvb[:, cc * 512:(cc + 1) * 512],
                                     start=(cc == 0), stop=(cc == CCN - 1))
                    return r
                op("tensor", vmm,
                   waits=[s_ve.at(vcast[(g, CCN - 1)]), s_sc.at(last_vev_sc[pb])],
                   inc=(s_pe.h, 1))
                pe_v = s_pe.inc()
                op("scalar", lambda e, pb=pb: e.copy(vev[pb][:], PSB[pb][:, :]),
                   waits=[s_pe.at(pe_v), s_dS.at(last_vev_dma[pb])], inc=(s_sc.h, 1))
                last_vev_sc[pb] = s_sc.inc()
                op("gpsimd", lambda e, g=g, tt=tt, pb=pb: e.dma_start(
                    out=v_dram[g * 4:(g + 1) * 4, tt * 128:(tt + 1) * 128, :]
                        .rearrange("h t hd -> t h hd"),
                    in_=vev[pb].rearrange("t (h hd) -> t h hd", h=4)),
                   waits=[s_sc.at(last_vev_sc[pb])], inc=(s_dS.h, 16))
                last_vev_dma[pb] = s_dS.inc(16)
            pe_end_g = s_pe.v
        barrier()

        cosb = ABF[:, 0:2048]
        sinb = ABF[:, 2048:4096]
        op("sync", lambda e: e.dma_start(out=cosb[:], in_=cos_in),
           inc=(s_dW.h, 16)); s_dW.inc(16)
        op("sync", lambda e: e.dma_start(out=sinb[:], in_=sin_in),
           inc=(s_dW.h, 16))
        dW_tab = s_dW.inc(16)
        wkb = [ABF[:, 4096:8192], ABF[:, 8192:12288]]
        qsb = [ABF[:, 12288:14336], ABF[:, 14336:16384]]
        qsw = ABF[:, 16384:18432]

        def emit_kq(w_src, dst_dram):
            last_wc = {0: s_ve.v, 1: s_ve.v}
            last_pe_w = [s_pe.v, s_pe.v]
            last_spill = [s_dS.v, s_dS.v]
            last_rope = [s_ve.v, s_ve.v]
            last_sc_ev = [s_sc.v, s_sc.v]
            last_swap = s_dV.v
            qc = {}
            for ohc in range(HL):
                par = ohc % 2
                for q in range(4):
                    bi = (ohc * 4 + q) % 2
                    op("sync", lambda e, ohc=ohc, q=q, bi=bi: e.dma_start(
                        out=wst[bi][:], in_=w_src[ohc * 4 + q, :, :]),
                       waits=[s_ve.at(last_wc[bi])], inc=(s_dW.h, 16))
                    dv = s_dW.inc(16)
                    op("vector", lambda e, par=par, q=q, bi=bi: e.tensor_copy(
                        wkb[par][:, q * 1024:(q + 1) * 1024], wst[bi][:]),
                       waits=[s_dW.at(dv), s_pe.at(last_pe_w[par])],
                       inc=(s_ve.h, 1))
                    last_wc[bi] = s_ve.inc()
                    qc[(ohc, q)] = last_wc[bi]
                bk = (0, 1, 2, 3) if par == 0 else (4, 5, 6, 7)

                def kmm(e, par=par, bk=bk):
                    for cc in range(CCN):
                        e.ldweights(wkb[par][:, cc * 128:(cc + 1) * 128])
                        for gg in range(4):
                            r = e.matmul(PSB[bk[gg]][:, :],
                                         wkb[par][:, cc * 128:(cc + 1) * 128],
                                         xn_full(cc)[:, gg * 512:(gg + 1) * 512],
                                         start=(cc == 0), stop=(cc == CCN - 1))
                    return r
                op("tensor", kmm,
                   waits=[s_ve.at(qc[(ohc, 3)]), s_sc.at(last_sc_ev[par])],
                   inc=(s_pe.h, 1))
                pe_k = s_pe.inc()
                last_pe_w[par] = pe_k

                def kev(e, par=par, bk=bk):
                    for gg in range(4):
                        r = e.copy(qsb[par][:, gg * 512:(gg + 1) * 512],
                                   PSB[bk[gg]][:, :])
                    return r
                op("scalar", kev,
                   waits=[s_pe.at(pe_k), s_dS.at(last_spill[par])],
                   inc=(s_sc.h, 1))
                last_sc_ev[par] = s_sc.inc()
                dump_raw = (debug and dst_dram is k_dram and ohc == 0)
                if dump_raw:
                    op("gpsimd", lambda e, par=par: e.dma_start(
                        out=dbg["d_kraw"], in_=qsb[par][:]),
                       waits=[s_sc.at(last_sc_ev[par])], inc=(s_dS.h, 16))
                    dump_raw_dv = s_dS.inc(16)
                op("scalar", lambda e, par=par: e.dma_start(
                    out=qsw[0:64, :], in_=qsb[par][64:128, :]),
                   waits=[s_ve.at(last_swap), s_sc.at(last_sc_ev[par])],
                   inc=(s_dV.h, 16))
                s_dV.inc(16)
                op("scalar", lambda e, par=par: e.dma_start(
                    out=qsw[64:128, :], in_=qsb[par][0:64, :]), inc=(s_dV.h, 16))
                dvv = s_dV.inc(16)
                if dump_raw:
                    op("gpsimd", lambda e: e.dma_start(
                        out=dbg["d_ksw"], in_=qsw[:]),
                       waits=[s_dV.at(dvv)], inc=(s_dS.h, 16))
                    dump_sw_dv = s_dS.inc(16)

                def rope(e, par=par):
                    e.tensor_tensor(qsb[par][:], qsb[par][:], cosb[:], op=ALU.mult)
                    e.tensor_tensor(qsw[:], qsw[:], sinb[:], op=ALU.mult)
                    return e.tensor_tensor(qsb[par][:], qsb[par][:], qsw[:],
                                           op=ALU.add)
                rope_w = [s_dV.at(dvv), s_dW.at(dW_tab),
                          s_dS.at(last_spill[par])]
                if dump_raw:
                    rope_w.append(s_dS.at(dump_sw_dv))
                op("vector", rope, waits=rope_w, inc=(s_ve.h, 1))
                last_rope[par] = s_ve.inc()
                last_swap = last_rope[par]
                op("gpsimd", lambda e, ohc=ohc, par=par: e.dma_start(
                    out=dst_dram[ohc, :, :], in_=qsb[par][:]),
                   waits=[s_ve.at(last_rope[par])], inc=(s_dS.h, 16))
                last_spill[par] = s_dS.inc(16)

        emit_kq(wkTp, k_dram)
        emit_kq(wqTp, q_dram)
        barrier()

        lg = S[2][0:8, 0:512]
        scrA = S[2][0:8, 512:1024]
        scrB = S[0][0:8, 0:512]
        bct = S[0][0:8, 512:1024]
        pg = S[1][0:8, 0:512]
        sig = S[1][0:8, 512:1024]
        m1r = S[3][0:1, 0:512]
        sumr = S[3][0:1, 512:1024]
        m1pr = hs_sb[0:1, 0:512]
        cntr = ds_sb[0:1, 0:512]
        m2r = S[3][0:1, 0:512]
        thr2r = m2r
        wsr = S[3][0:1, 512:1024]
        rc = [s_ve.v]

        def vop(fn, extra=()):
            op("vector", fn, waits=[s_ve.at(rc[0])] + list(extra),
               inc=(s_ve.h, 1))
            rc[0] = s_ve.inc()

        def preduce(X, out_row, alu):
            for n in (4, 2, 1):
                src_t = X if n == 4 else scrB
                dst = out_row if n == 1 else scrB
                vop(lambda e: e.tensor_copy(XN[:, 65024:65280],
                                            XN[:, 65280:65536]))
                op("scalar", lambda e, n=n, src_t=src_t: e.dma_start(
                    out=scrA[0:n, :], in_=src_t[n:2 * n, 0:512]),
                   waits=[s_ve.at(rc[0])], inc=(s_dV.h, 16))
                dvv = s_dV.inc(16)
                vop(lambda e, n=n, src_t=src_t, dst=dst, alu=alu:
                    e.tensor_tensor(dst[0:n, 0:512] if n > 1 else dst,
                                    src_t[0:n, 0:512], scrA[0:n, :], op=alu),
                    extra=[s_dV.at(dvv)])

        def bcast8(row):
            op("tensor", lambda e, row=row: e.matmul(
                PSB[0][0:8, :], ones_rowf[0:1, 0:8], row, start=True, stop=True),
               waits=[s_ve.at(rc[0])], inc=(s_pe.h, 1))
            pv = s_pe.inc()
            vop(lambda e: e.tensor_copy(bct[:], PSB[0][0:8, :]),
                extra=[s_pe.at(pv)])

        op("gpsimd", lambda e: e.dma_start(
            out=agsb[0:1, 0:64], in_=ag_out.rearrange("a b -> (a b)")),
           waits=[s_cg.at(1)], inc=(s_dS.h, 16))
        dv = s_dS.inc(16)

        def gmax2(e):
            e.tensor_reduce(mx2[0:1, 0:1], agsb[0:1, 0:64:8], axis=AX.X, op=ALU.max)
            e.tensor_reduce(mx2[0:1, 1:2], agsb[0:1, 1:64:8], axis=AX.X, op=ALU.max)
            e.tensor_copy(XN[:, 65024:65280], XN[:, 65280:65536])
            e.tensor_scalar(mx2[:], mx2[:], 1e-6, None, op0=ALU.max)
            return e.tensor_copy(XN[:, 65024:65280], XN[:, 65280:65536])
        vop(gmax2, extra=[s_dS.at(dv)])
        op("tensor", lambda e: e.matmul(PSB[0][0:8, 0:2], ones8[:], mx2[0:1, 0:2],
                                        start=True, stop=True),
           waits=[s_ve.at(rc[0])], inc=(s_pe.h, 1))
        pe_g = s_pe.inc()

        def rscale(e):
            e.tensor_copy(sc2[:], PSB[0][0:8, 0:2])
            e.tensor_copy(XN[:, 65024:65280], XN[:, 65280:65536])
            e.reciprocal(sc2[:], sc2[:])
            return e.tensor_copy(XN[:, 65024:65280], XN[:, 65280:65536])
        vop(rscale, extra=[s_pe.at(pe_g)])

        def logits(e):
            e.tensor_scalar(lg[:], hs_sb[:], sc2[:, 0:1], 0.5,
                            op0=ALU.mult, op1=ALU.mult)
            e.tensor_scalar(scrA[:], ds_sb[:], sc2[:, 1:2], 0.5,
                            op0=ALU.mult, op1=ALU.mult)
            return e.tensor_tensor(lg[:], lg[:], scrA[:], op=ALU.add)
        vop(logits)
        preduce(lg, m1r, ALU.max)
        bcast8(m1r)
        vop(lambda e: e.tensor_tensor(scrA[:], lg[:], bct[:], op=ALU.subtract))
        op("scalar", lambda e: e.activation(pg[:], scrA[:], AF.Exp),
           waits=[s_ve.at(rc[0])], inc=(s_sc.h, 1))
        sc_e = s_sc.inc()
        vop(lambda e: e.nop(), extra=[s_sc.at(sc_e)])
        preduce(pg, sumr, ALU.add)
        vop(lambda e: e.reciprocal(sumr, sumr))
        bcast8(sumr)
        vop(lambda e: e.tensor_tensor(pg[:], pg[:], bct[:], op=ALU.mult))
        preduce(pg, m1pr, ALU.max)
        bcast8(m1pr)
        eqt = S[2][0:8, 0:512]
        vop(lambda e: e.tensor_tensor(eqt[:], pg[:], bct[:], op=ALU.is_ge))
        preduce(eqt, cntr, ALU.add)
        vop(lambda e: e.scalar_tensor_tensor(eqt[:], eqt[:], -1e30, pg[:],
                                             op0=ALU.mult, op1=ALU.add))
        preduce(eqt, m2r, ALU.max)

        def rows(e):
            e.tensor_scalar(cntr, cntr, 1.5, None, op0=ALU.is_ge)
            e.tensor_tensor(m1pr, cntr, m1pr, op=ALU.mult)
            e.tensor_tensor(m2r, m2r, m1pr, op=ALU.max)
            return e.tensor_scalar(thr2r, m2r, -2.0, None, op0=ALU.mult)
        vop(rows)
        bcast8(thr2r)
        vop(lambda e: e.scalar_tensor_tensor(scrA[:], pg[:], 2.0, bct[:],
                                             op0=ALU.mult, op1=ALU.add))
        op("scalar", lambda e: e.activation(sig[:], scrA[:], AF.Sigmoid),
           waits=[s_ve.at(rc[0])], inc=(s_sc.h, 1))
        sc_s = s_sc.inc()
        wnF = sig
        vop(lambda e: e.tensor_tensor(wnF[:], pg[:], sig[:], op=ALU.mult),
            extra=[s_sc.at(sc_s)])
        preduce(wnF, wsr, ALU.add)

        def wsf(e):
            e.tensor_scalar(wsr, wsr, 1e-9, None, op0=ALU.add)
            return e.reciprocal(wsr, wsr)
        vop(wsf)
        bcast8(wsr)
        vop(lambda e: e.tensor_tensor(wnF[:], wnF[:], bct[:], op=ALU.mult))
        if debug:
            op("gpsimd", lambda e: e.dma_start(out=dbg["d_wn"], in_=wnF[:]),
               waits=[s_ve.at(rc[0])], inc=(s_dS.h, 16))
            s_dS.inc(16)
        wful = S[2][0:8, 0:1024]
        wfulb = ABF[0:8, 12800:13824]

        def wrep(e):
            e.tensor_copy(wful[:, 0:1024:2], wnF[:])
            e.tensor_copy(wful[:, 1:1024:2], wnF[:])
            return e.tensor_copy(wfulb[:], wful[:])
        vop(wrep)

        def pmm(e):
            e.ldweights(Lmb[:])
            e.matmul(PSB[3][0:32, :], Lmb[:], wfulb[:, 0:512], start=True, stop=True)
            return e.matmul(PSB[4][0:32, :], Lmb[:], wfulb[:, 512:1024],
                            start=True, stop=True)
        op("tensor", pmm, waits=[s_ve.at(rc[0])], inc=(s_pe.h, 1))
        pe_pm = s_pe.inc()
        pmx = S[0][0:32, :]

        def zpf(e):
            e.tensor_copy(pmx[:, 0:512], PSB[3][0:32, :])
            e.tensor_copy(pmx[:, 512:1024], PSB[4][0:32, :])
            e.tensor_tensor(pmx[:], z_sb[:], pmx[:], op=ALU.mult)
            return e.tensor_copy(zpb[:], pmx[:])
        vop(zpf, extra=[s_pe.at(pe_pm)])
        ve_routing_end = rc[0]

        kh = [ABF[:, 0:2048], ABF[:, 2048:4096]]
        qh = [ABF[:, 4096:6144], ABF[:, 6144:8192]]
        vh = [ABF[:, 8192:10240], ABF[:, 10240:12288]]
        pt = [ABF[:, 12288:13312], ABF[:, 13312:14336]]
        srow = S[2][0:1, :]
        rsb = S[3]
        o_view = lambda ohc, qhalf: XN[:, ohc * T + qhalf * MY:
                                       ohc * T + qhalf * MY + MY]
        last_load = [s_dS.v, s_dS.v]
        last_head_pe = [s_pe.v, s_pe.v]
        last_pt = [s_sc.v, s_sc.v]
        pt_guard = [s_pe.v, s_pe.v]
        last_onorm = s_ve.v
        last_rsb = s_sc.v
        for ohc in range(HL):
            par = ohc % 2
            for dst, src in [
                (kh[par][:], k_dram[ohc, :, :]),
                (qh[par][:], q_dram[ohc, :, :]),
                (vh[par].rearrange("p (tt hd) -> p tt hd", tt=16),
                 v_dram[ohc, :, :].rearrange("(tt p) hd -> p tt hd", p=128)),
            ]:
                op("gpsimd", lambda e, d=dst, s=src: e.dma_start(out=d, in_=s),
                   waits=[s_pe.at(last_head_pe[par])], inc=(s_dS.h, 16))
                s_dS.inc(16)
            last_load[par] = s_dS.v
            for qhalf in range(2):
                nk = 8 if qhalf == 0 else 16
                sums_pe_val = {}
                pending = None
                for idx in range(nk):
                    i = idx
                    c0 = (i * 128) if qhalf == 0 else max(i - 8, 0) * 128
                    h0 = max(c0 - 512, 0)
                    stp = (0, 1) if idx % 2 == 0 else (2, 3)
                    masked = (qhalf == 0) or (i >= 8)

                    def stmm(e, par=par, i=i, c0=c0, h0=h0, stp=stp,
                             qhalf=qhalf):
                        e.ldweights(kh[par][:, i * 128:(i + 1) * 128])
                        q0 = qhalf * 1024
                        if c0 < 512:
                            e.matmul(PSB[stp[0]][:, c0:512],
                                     kh[par][:, i * 128:(i + 1) * 128],
                                     qh[par][:, q0 + c0:q0 + 512],
                                     start=True, stop=True)
                        return e.matmul(PSB[stp[1]][:, h0:512],
                                        kh[par][:, i * 128:(i + 1) * 128],
                                        qh[par][:, q0 + 512 + h0:q0 + 1024],
                                        start=True, stop=True)
                    extra_w = ([s_ve.at(ve_routing_end)]
                               if (ohc == 0 and qhalf == 0 and idx == 0) else [])
                    op("tensor", stmm,
                       waits=[s_dS.at(last_load[par]), s_sc.at(last_pt[idx % 2])]
                             + extra_w,
                       inc=(s_pe.h, 1))
                    pe_st = s_pe.inc()
                    if pending is not None:
                        op("tensor", pending[0], waits=pending[1],
                           inc=(s_pe.h, 1))
                        sums_pe_val[idx - 1] = s_pe.inc()
                        pt_guard[(idx - 1) % 2] = sums_pe_val[idx - 1]
                        pending = None

                    def expf(e, c0=c0, h0=h0, stp=stp, pb=idx % 2):
                        if c0 < 512:
                            e.activation(pt[pb][:, c0:512], PSB[stp[0]][:, c0:512],
                                         AF.Exp, bias=-20.0,
                                         scale=0.08838834764831845)
                        return e.activation(pt[pb][:, 512 + h0:1024],
                                            PSB[stp[1]][:, h0:512],
                                            AF.Exp, bias=-20.0,
                                            scale=0.08838834764831845)
                    op("scalar", expf,
                       waits=[s_pe.at(pe_st), s_pe.at(pt_guard[idx % 2])],
                       inc=(s_sc.h, 1))
                    last_pt[idx % 2] = s_sc.inc()
                    wl = [s_sc.at(last_pt[idx % 2])]
                    if masked:
                        op("vector", lambda e, c0=c0, pb=idx % 2: e.tensor_tensor(
                            pt[pb][:, c0:c0 + 128], pt[pb][:, c0:c0 + 128],
                            mask_sb[:], op=ALU.mult),
                           waits=wl, inc=(s_ve.h, 1))
                        wl = [s_ve.at(s_ve.inc())]
                    if debug and ohc == 0 and qhalf == 0 and idx == 0:
                        op("gpsimd", lambda e: e.dma_start(
                            out=dbg["d_pt0"], in_=pt[0][:]),
                           waits=list(wl), inc=(s_dS.h, 16))
                        pt_dump_dv = [s_dS.inc(16)]
                        wl = wl + [s_dS.at(pt_dump_dv[0])]
                    if idx == 0:
                        wl = wl + [s_ve.at(last_onorm)]

                    def sums_pv(e, par=par, i=i, c0=c0, h0=h0, idx=idx,
                                pb=idx % 2):
                        st = (idx == 0)
                        e.ldweights(onesc[:])
                        if c0 < 512:
                            e.matmul(PSB[6][0:1, c0:512], onesc[:],
                                     pt[pb][:, c0:512], start=st, stop=True)
                        e.matmul(PSB[7][0:1, h0:512], onesc[:],
                                 pt[pb][:, 512 + h0:1024], start=st, stop=True)
                        e.ldweights(vh[par][:, i * 128:(i + 1) * 128])
                        if c0 < 512:
                            e.matmul(PSB[4][:, c0:512],
                                     vh[par][:, i * 128:(i + 1) * 128],
                                     pt[pb][:, c0:512], start=st, stop=True)
                        return e.matmul(PSB[5][:, h0:512],
                                        vh[par][:, i * 128:(i + 1) * 128],
                                        pt[pb][:, 512 + h0:1024],
                                        start=st, stop=True)
                    pending = (sums_pv, wl)
                    if not ATT_PIPE:
                        op("tensor", pending[0], waits=pending[1],
                           inc=(s_pe.h, 1))
                        sums_pe_val[idx] = s_pe.inc()
                        pt_guard[idx % 2] = sums_pe_val[idx]
                        pending = None
                if pending is not None:
                    op("tensor", pending[0], waits=pending[1], inc=(s_pe.h, 1))
                    sums_pe_val[nk - 1] = s_pe.inc()
                    pt_guard[(nk - 1) % 2] = sums_pe_val[nk - 1]
                    pending = None
                pe_kloop = s_pe.v

                def sum_ev(e):
                    e.tensor_copy(srow[:, 0:512], PSB[6][0:1, :])
                    e.tensor_copy(srow[:, 512:1024], PSB[7][0:1, :])
                    return e.reciprocal(srow[:], srow[:])
                op("vector", sum_ev, waits=[s_pe.at(pe_kloop)], inc=(s_ve.h, 1))
                ve_sum = s_ve.inc()
                if debug and ohc == 0 and qhalf == 0:
                    op("gpsimd", lambda e: e.dma_start(
                        out=dbg["d_sum"], in_=srow[:]),
                       waits=[s_ve.at(ve_sum)], inc=(s_dS.h, 16))
                    sum_dump_dv = s_dS.inc(16)
                    op("vector", lambda e: e.nop(),
                       waits=[s_dS.at(sum_dump_dv)], inc=(s_ve.h, 1))
                    ve_sum = s_ve.inc()

                def rbc(e):
                    e.matmul(PSB[0][:, :], ones_rowf[:], srow[:, 0:512],
                             start=True, stop=True)
                    return e.matmul(PSB[1][:, :], ones_rowf[:], srow[:, 512:1024],
                                    start=True, stop=True)
                op("tensor", rbc,
                   waits=[s_ve.at(ve_sum), s_sc.at(last_pt[0]),
                          s_sc.at(last_pt[1])], inc=(s_pe.h, 1))
                pe_rbc = s_pe.inc()

                def rsbc(e):
                    e.copy(rsb[:, 0:512], PSB[0][:, :])
                    return e.copy(rsb[:, 512:1024], PSB[1][:, :])
                op("scalar", rsbc, waits=[s_pe.at(pe_rbc), s_ve.at(last_onorm)],
                   inc=(s_sc.h, 1))
                last_rsb = s_sc.inc()
                last_pt[0] = last_rsb

                def onorm(e, ohc=ohc, qhalf=qhalf):
                    ov = o_view(ohc, qhalf)
                    e.tensor_tensor(ov[:, 0:512], PSB[4][:, :], rsb[:, 0:512],
                                    op=ALU.mult)
                    return e.tensor_tensor(ov[:, 512:1024], PSB[5][:, :],
                                           rsb[:, 512:1024], op=ALU.mult)
                op("vector", onorm, waits=[s_sc.at(last_rsb)], inc=(s_ve.h, 1))
                last_onorm = s_ve.inc()
            last_head_pe[par] = s_pe.v
        barrier()

        wob = [ABF[:, 0:2048], ABF[:, 2048:4096]]
        ssb = AFR[:, 2048:4096]
        last_wc = {0: s_ve.v, 1: s_ve.v}
        last_pe_w = [s_pe.v, s_pe.v]
        last_ssb_dma = s_dS.v
        last_ssb_sc = s_sc.v
        oqc = {}
        for oc in range(CCN):
            par = oc % 2
            for q in range(2):
                bi = (oc * 2 + q) % 2
                op("sync", lambda e, oc=oc, q=q, bi=bi: e.dma_start(
                    out=wst[bi][:], in_=woTp[oc * 2 + q, :, :]),
                   waits=[s_ve.at(last_wc[bi])], inc=(s_dW.h, 16))
                dv = s_dW.inc(16)
                op("vector", lambda e, par=par, q=q, bi=bi: e.tensor_copy(
                    wob[par][:, q * 1024:(q + 1) * 1024], wst[bi][:]),
                   waits=[s_dW.at(dv), s_pe.at(last_pe_w[par])], inc=(s_ve.h, 1))
                last_wc[bi] = s_ve.inc()
                oqc[(oc, q)] = last_wc[bi]
            bk = (0, 1, 2, 3) if par == 0 else (4, 5, 6, 7)

            def omm(e, oc=oc, par=par, bk=bk):
                for cc in range(HL):
                    e.ldweights(wob[par][:, cc * 128:(cc + 1) * 128])
                    for gg in range(4):
                        r = e.matmul(PSB[bk[gg]][:, :],
                                     wob[par][:, cc * 128:(cc + 1) * 128],
                                     XN[:, cc * T + gg * 512: cc * T + (gg + 1) * 512],
                                     start=(cc == 0), stop=(cc == HL - 1))
                return r
            op("tensor", omm,
               waits=[s_ve.at(oqc[(oc, 1)]), s_sc.at(last_ssb_sc)],
               inc=(s_pe.h, 1))
            pe_o = s_pe.inc()
            last_pe_w[par] = pe_o

            def oev(e, bk=bk):
                for gg in range(4):
                    r = e.copy(ssb[:, gg * 512:(gg + 1) * 512], PSB[bk[gg]][:, :])
                return r
            op("scalar", oev, waits=[s_pe.at(pe_o), s_dS.at(last_ssb_dma)],
               inc=(s_sc.h, 1))
            last_ssb_sc = s_sc.inc()
            op("gpsimd", lambda e, oc=oc: e.dma_start(
                out=sp_in.rearrange("(h of) t -> h of t", h=2)
                    [:, oc * 128:(oc + 1) * 128, :]
                    .rearrange("h of t -> of h t"),
                in_=ssb.rearrange("p (h t) -> p h t", h=2)),
               waits=[s_sc.at(last_ssb_sc)], inc=(s_dS.h, 16))
            last_ssb_dma = s_dS.inc(16)
        dv = s_dS.v
        op("gpsimd", lambda e: e.collective_compute(
            "ReduceScatter", ALU.add,
            replica_groups=[[0, 1], [2, 3], [4, 5], [6, 7]],
            ins=[sp_in_t.ap().opt()], outs=[sp_out_t.ap().opt()]),
           waits=[s_dS.at(dv)], inc=(s_cg.h, 1))
        s_cg.inc()
        barrier()

        Bmb = XN[0:32, 40960:45056]
        for hq in range(2):
            op("sync", lambda e, hq=hq: e.dma_start(
                out=AFR[0:32, hq * 2048:(hq + 1) * 2048],
                in_=Bm_in[:, hq * 2048:(hq + 1) * 2048]), inc=(s_dW.h, 16))
            s_dW.inc(16)
        dW_bm = s_dW.v
        op("vector", lambda e: e.tensor_copy(Bmb[:], AFR[0:32, 0:4096]),
           waits=[s_dW.at(dW_bm)], inc=(s_ve.h, 1))
        ve_bmb = s_ve.inc()
        barrier()
        hld = [S[0], S[1]]
        last_h = [s_sc.v, s_sc.v]
        last_g = [s_dS.v, s_dS.v]
        last_d_pe = [s_ve.v, s_ve.v]
        for oc in range(CCN):
            par = oc % 2
            op("sync", lambda e, oc=oc, par=par: e.dma_start(
                out=hld[par][:], in_=h_dram[oc * 128:(oc + 1) * 128, :]),
               waits=[s_sc.at(last_h[par]), s_dS.at(last_g[par])],
               inc=(s_dW.h, 16))
            d_h = s_dW.inc(16)
            pD = (0, 1) if par == 0 else (2, 3)

            def dmm(e, oc=oc, pD=pD):
                e.ldweights(Bmb[:, oc * 128:(oc + 1) * 128])
                e.matmul(PSB[pD[0]][:, :], Bmb[:, oc * 128:(oc + 1) * 128],
                         zpb[:, 0:512], start=True, stop=True)
                return e.matmul(PSB[pD[1]][:, :], Bmb[:, oc * 128:(oc + 1) * 128],
                                zpb[:, 512:1024], start=True, stop=True)
            op("tensor", dmm, waits=[s_ve.at(ve_bmb), s_ve.at(last_d_pe[par])],
               inc=(s_pe.h, 1))
            pe_d = s_pe.inc()

            def hd(e, oc=oc, par=par, pD=pD):
                e.tensor_scalar(hld[par][:], hld[par][:], b_sb[:, oc:oc + 1],
                                None, op0=ALU.add)
                e.scalar_tensor_tensor(hld[par][:, 0:512], PSB[pD[0]][:, :], 0.5,
                                       hld[par][:, 0:512], op0=ALU.mult,
                                       op1=ALU.add)
                return e.scalar_tensor_tensor(hld[par][:, 512:1024],
                                              PSB[pD[1]][:, :], 0.5,
                                              hld[par][:, 512:1024],
                                              op0=ALU.mult, op1=ALU.add)
            op("vector", hd, waits=[s_pe.at(pe_d), s_dW.at(d_h)], inc=(s_ve.h, 1))
            ve_hd = s_ve.inc()
            last_d_pe[par] = ve_hd
            op("scalar", lambda e, par=par: e.activation(
                hld[par][:], hld[par][:], AF.Silu),
               waits=[s_ve.at(ve_hd), s_dS.at(last_g[par])], inc=(s_sc.h, 1))
            sc_g = s_sc.inc()
            last_h[par] = sc_g
            op("gpsimd", lambda e, oc=oc, par=par: e.dma_start(
                out=g_dram[oc * 128:(oc + 1) * 128, :], in_=hld[par][:]),
               waits=[s_sc.at(sc_g)], inc=(s_dS.h, 16))
            last_g[par] = s_dS.inc(16)
        gld = [S[0], S[1]]
        xld = [S[2], S[3]]
        sld = S[4]
        last_fin = [s_dS.v, s_dS.v]
        last_sv = s_ve.v
        for oc in range(CCN):
            par = oc % 2
            wg = [s_dS.at(last_g[par]), s_dS.at(last_fin[par])]
            op("sync", lambda e, oc=oc, par=par: e.dma_start(
                out=gld[par][:], in_=g_dram[oc * 128:(oc + 1) * 128, :]),
               waits=wg, inc=(s_dW.h, 16))
            d_g = s_dW.inc(16)
            op("sync", lambda e, oc=oc, par=par: e.dma_start(
                out=xld[par][:], in_=x_myT[oc * 128:(oc + 1) * 128, :]),
               waits=[s_dS.at(last_fin[par])], inc=(s_dW.h, 16))
            d_x = s_dW.inc(16)
            op("gpsimd", lambda e, oc=oc: e.dma_start(
                out=sld[:], in_=sp_out[oc * 128:(oc + 1) * 128, :]),
               waits=[s_cg.at(2), s_ve.at(last_sv)], inc=(s_dS.h, 16))
            d_s = s_dS.inc(16)

            def fin(e, par=par):
                e.tensor_tensor(gld[par][:], sld[:], gld[par][:], op=ALU.mult)
                return e.tensor_tensor(gld[par][:], xld[par][:], gld[par][:],
                                       op=ALU.add)
            op("vector", fin,
               waits=[s_dW.at(d_g), s_dW.at(d_x), s_dS.at(d_s)],
               inc=(s_ve.h, 1))
            ve_f = s_ve.inc()
            last_sv = ve_f
            op("gpsimd", lambda e, oc=oc, par=par: e.dma_start(
                out=out_fm[oc * 128:(oc + 1) * 128, :], in_=gld[par][:]),
               waits=[s_ve.at(ve_f)], inc=(s_dS.h, 16))
            last_fin[par] = s_dS.inc(16)

        if debug:
            for nm, sap in [("d_g0", g_dram[0:128, :]), ("d_g1", g_dram[128:256, :]),
                            ("d_k0", k_dram[0, :, :]), ("d_q0", q_dram[0, :, :]),
                            ("d_v0", v_dram[0, :, :]), ("d_o0", XN[:, 0:T]),
                            ("d_H0", h_dram[0:128, :]), ("d_sp", sp_out[0:128, :]),
                            ("d_sc", mx2[0:1, 0:2]), ("d_zpb", zpb[:]),
                            ("d_rr", rr2[:]), ("d_ag", agsb[:]),
                            ("d_hs", hs_sb[:])]:
                op("gpsimd", lambda e, d=dbg[nm], s=sap: e.dma_start(out=d, in_=s),
                   waits=[(s_ve.h, s_ve.v), (s_sc.h, s_sc.v)], inc=(s_dS.h, 16))
                s_dS.inc(16)

        fin_vals = [(s_dS.h, s_dS.v), (s_dW.h, s_dW.v), (s_dV.h, s_dV.v)]
        op("gpsimd", lambda e: None, waits=fin_vals)

        @block.sync
        def _(sync):
            for f in plan["sync"]:
                f(sync)

        @block.vector
        def _(vector):
            for f in plan["vector"]:
                f(vector)

        @block.scalar
        def _(scalar):
            for f in plan["scalar"]:
                f(scalar)

        @block.tensor
        def _(tensor):
            for f in plan["tensor"]:
                f(tensor)

        @block.gpsimd
        def _(gpsimd):
            for f in plan["gpsimd"]:
                f(gpsimd)

    return nc


_PERM = None


def _host_inputs(L0_attn_x, L0_attn_wq, L0_attn_wk, L0_attn_wv, L0_attn_wo,
                 L0_ffn_W, L0_ffn_b, L0_ffn_A, L0_ffn_Bmat, L0_ffn_LiMEs):
    global _PERM
    if _PERM is None:
        r = np.concatenate([np.arange(0, 128, 2), np.arange(1, 128, 2)])
        _PERM = np.concatenate([h * 128 + r for h in range(32)])
    perm = _PERM
    f32 = np.float32
    bf16 = ml_dtypes.bfloat16
    wqT_f = L0_attn_wq[perm].T
    wkT_f = L0_attn_wk[perm].T
    wvT = np.ascontiguousarray(L0_attn_wv.T)
    woT_f = L0_attn_wo.T
    WT_f = L0_ffn_W.T

    def tile_w(wt, nblk_rows):
        R, OF = wt.shape
        nq = R // 1024
        ofn = OF // 128
        a = wt.reshape(nq, 8, 128, ofn, 128)
        a = a.transpose(3, 0, 2, 1, 4)
        return np.ascontiguousarray(a.reshape(ofn * nq, 128, 1024))

    WTp = tile_w(WT_f, None)
    b_re = np.ascontiguousarray(L0_ffn_b.reshape(32, 128).T)
    LmT = np.ascontiguousarray(L0_ffn_LiMEs.T)
    t = np.arange(T, dtype=np.float64)
    cosb = np.broadcast_to(np.cos(t).astype(f32), (128, T)).astype(bf16)
    sin_t = np.sin(t).astype(f32)
    sinb = np.concatenate([np.broadcast_to(-sin_t, (64, T)),
                           np.broadcast_to(sin_t, (64, T))], axis=0).astype(bf16)
    mask01 = np.triu(np.ones((128, 128), dtype=f32)).astype(bf16)
    ident = np.eye(128, dtype=f32)
    in_maps = []
    for c in range(8):
        bb, hh = c // 2, c % 2
        xb = L0_attn_x[bb]
        x_fmT = np.ascontiguousarray(xb.T)
        x_myT = np.ascontiguousarray(xb[hh * MY:(hh + 1) * MY].T)
        of = slice(hh * 2048, (hh + 1) * 2048)
        in_maps.append({
            "x_fmT": x_fmT.astype(f32),
            "x_myT": x_myT.astype(f32),
            "wvT": np.ascontiguousarray(wvT[:, of]),
            "wkTp": tile_w(wkT_f[:, of], None),
            "wqTp": tile_w(wqT_f[:, of], None),
            "woTp": tile_w(np.ascontiguousarray(woT_f[hh * 2048:(hh + 1) * 2048, :]), None),
            "WTp": WTp,
            "b_re": b_re,
            "A_in": np.ascontiguousarray(L0_ffn_A),
            "Bm_in": np.ascontiguousarray(L0_ffn_Bmat),
            "Lm_in": np.ascontiguousarray(L0_ffn_LiMEs),
            "LmT_in": LmT,
            "cos_in": cosb,
            "sin_in": sinb,
            "mask_in": mask01,
            "ident_in": ident,
        })
    return in_maps


def kernel(**inputs):
    in_maps = _host_inputs(**inputs)
    nc = build_nc(debug=DEBUG)
    res = run_bass_kernel_spmd(nc, in_maps, core_ids=list(range(8)))
    out = np.empty((4, T, C), dtype=np.float32)
    for c in range(8):
        bb, hh = c // 2, c % 2
        of = np.asarray(res.results[c]["out_fm"], dtype=np.float32)
        out[bb, hh * MY:(hh + 1) * MY, :] = of.T
    if DEBUG:
        kernel.last_res = res
    return out
